# version 21
# baseline (speedup 1.0000x reference)
import numpy as np
import ml_dtypes
import concourse.bass as bass
import concourse.mybir as mybir
from concourse.bass_utils import run_bass_kernel_spmd

F32 = mybir.dt.float32
BF16 = mybir.dt.bfloat16
FP8 = mybir.dt.float8e4
I8 = mybir.dt.int8
AF = mybir.ActivationFunctionType
ALU = mybir.AluOpType

T = 2048
C = 1024
H = 16
D = 64
KC = 8
NBLK = 4
TPB = 4
LN_EPS = 1e-5
ATT_EPS = 1e-6


class _Sem:
    def __init__(self, handle):
        self.h = handle
        self.count = 0


class _Builder:
    def __init__(self):
        self.eng_ops = {e: [] for e in ("sync", "scalar", "vector", "gpsimd", "tensor")}

    def op(self, eng, fn, waits=(), incs=()):
        for s, v in waits:
            assert v <= s.count, f"wait {v} > produced {s.count} ({eng})"
        self.eng_ops[eng].append((list(waits), fn, list(incs)))
        for s, n in incs:
            s.count += n


def build_nc():
    nc = bass.Bass("TRN2", target_bir_lowering=False, detect_race_conditions=False)

    xh = nc.dram_tensor("x", [T, C], F32, kind="ExternalInput").ap()
    Wa = nc.dram_tensor("W_attn", [C, 3 * C], F32, kind="ExternalInput").ap()
    ba = nc.dram_tensor("b_attn", [3 * C], F32, kind="ExternalInput").ap()
    Wp = nc.dram_tensor("W_proj", [C, C], F32, kind="ExternalInput").ap()
    bp = nc.dram_tensor("b_proj", [C], F32, kind="ExternalInput").ap()
    Wft = nc.dram_tensor("Wt_ffn", [C, C], F32, kind="ExternalInput").ap()
    bf = nc.dram_tensor("b_ffn", [C], F32, kind="ExternalInput").ap()
    maskh = nc.dram_tensor("mask", [128, 128], BF16, kind="ExternalInput").ap()
    identh = nc.dram_tensor("ident", [128, 128], BF16, kind="ExternalInput").ap()
    bqkh = nc.dram_tensor("bqk_h", [128, 16], F32, kind="ExternalInput").ap()
    outh = nc.dram_tensor("out", [T, C], F32, kind="ExternalOutput").ap()

    import contextlib
    ctx = contextlib.ExitStack()
    sb = lambda name, shape, dt: ctx.enter_context(nc.sbuf_tensor(name, shape, dt))
    ps = lambda name, shape, dt: ctx.enter_context(nc.psum_tensor(name, shape, dt))
    sem = lambda name: _Sem(ctx.enter_context(nc.semaphore(name)))

    with ctx:
        W_qk = sb("W_qk", [128, KC, 2 * C], BF16)
        W_v = sb("W_v", [128, KC, C], BF16)
        W_pr = sb("W_pr", [128, KC, C], BF16)
        W_f8 = sb("W_f8", [128, KC, C], BF16)
        bqk = sb("bqk", [128, 16], F32)
        brows = sb("brows", [65, C], BF16)
        bv_row = brows[0:1, :]
        bp_row = brows[32:33, :]
        bf_row = brows[64:65, :]
        ones_row = sb("ones_row", [65, 128], BF16)
        ones_col = sb("ones_col", [128, 1], BF16)
        zero_row = sb("zero_row", [1, 128], BF16)
        mask_rep = sb("mask_rep", [128, 128], BF16)
        ident_sb = sb("ident_sb", [128, 128], BF16)
        x_sb = sb("x_sb", [128, 2, TPB, C], F32)
        r_sb = sb("r_sb", [128, TPB, C], F32)
        ln_tm = sb("ln_tm", [128, TPB, C], BF16)
        ln_fm = sb("ln_fm", [128, TPB, KC, 128], BF16)
        y_fm1 = sb("y_fm1", [128, KC, 128], BF16)
        q_fm = sb("q_fm", [128, TPB, KC, 128], BF16)
        k_fm = sb("k_fm", [128, TPB, KC, 128], BF16)
        k_tm = sb("k_tm", [128, TPB, C], BF16)
        q8_fm = sb("q8_fm", [128, 2, KC, 128], BF16)
        v_sb = sb("v_sb", [128, TPB, H * 65], BF16)
        k_mk = sb("k_mk", [128, 2, KC, 128], BF16)
        stm = sb("stm", [128, H, 128], BF16)
        y_tm = sb("y_tm", [128, C], BF16)
        q8_tm = sb("q8_tm", [128, TPB, C], I8)
        e_sb = sb("e_sb", [128, 2, 512], BF16)
        kva = sb("kva", [128, 2, H // 2, D + 1], BF16)
        stats = []
        for tag in ("a", "b"):
            stats.append(dict(
                bn12=sb(f"bn12{tag}", [128, TPB, 12], F32),
                mvab=sb(f"mvab{tag}", [128, TPB, 4], F32),
                mvs4=sb(f"mvs4{tag}", [128, TPB, 2], F32),
                mvd4=sb(f"mvd4{tag}", [128, TPB, 2], F32),
                mvp4=sb(f"mvp4{tag}", [128, TPB, 2], F32),
                mean4=sb(f"mean4{tag}", [128, TPB], F32),
                var4=sb(f"var4{tag}", [128, TPB], F32),
                sqv=sb(f"sqv{tag}", [128, TPB], F32),
                invs=sb(f"invs{tag}", [128, TPB], F32),
                nmb=sb(f"nmb{tag}", [128, TPB], F32),
                sqrt_act=0,
            ))
        den_f = sb("den_f", [128, 16], F32)
        rcp_f = sb("rcp_f", [128, 16], F32)
        c_eps5 = sb("c_eps5", [128, 1], F32)
        c_eps6 = sb("c_eps6", [128, 1], F32)
        c_m1 = sb("c_m1", [128, 1], F32)
        nc.const_aps.register(F32, LN_EPS, c_eps5[:])
        nc.const_aps.register(F32, ATT_EPS, c_eps6[:])
        nc.const_aps.register(F32, -1.0, c_m1[:])
        pbig = [ps("pbig0", [128, 512], F32), ps("pbig1", [128, 512], F32)]
        pnd = ps("pnd", [128, 1536], F32)
        pkv = ps("pkv", [128, 1024], F32)

        s_x = [sem("s_x0"), sem("s_x1")]
        s_const = sem("s_const")
        s_w = sem("s_w")
        s_wv = sem("s_wv")
        s_wqk = sem("s_wqk")
        s_wpr = sem("s_wpr")
        s_stg = [sem("s_stg0"), sem("s_stg1")]
        s_trl = sem("s_trl")
        s_trk = sem("s_trk")
        s_try = sem("s_try")
        s_trq = sem("s_trq")
        s_mm = sem("s_mm")
        s_act = sem("s_act")
        s_dve = sem("s_dve")
        s_pl = sem("s_pl")
        s_w8 = sem("s_w8")
        s_out = sem("s_out")

        B = _Builder()
        op = B.op

        def _koff(s):
            return 65 * s if s < 7 else 512

        def _noff(h):
            if h < 7:
                return 65 * h
            if h < 14:
                return 512 + 65 * (h - 7)
            return 1024 + 65 * (h - 14)

        st = dict(
            x_loaded={}, out_done={}, ln_fm_rdy={}, ln_tm_free={}, ktm_tr={}, ytr={},
            ptile_mm={}, qk_mm={}, q8_cp={}, ffn_mm_t={}, q8_tr_t={},
            qk_ready={}, k_ready={}, v_ready={}, q8_tr={}, q8_act={},
            attn_mm={}, proj_mm={}, ffn_mm={}, nm_dve={},
            phi_hist=[], div_dve=0, kva_act=0, ks_dve=0, ksum_read_act={},
            chunk_mm=0, w8_all=0, w8_act_all=0,
        )
        bank_free = [[], []]

        op("vector", lambda v: v.memset(ones_row[:], 1.0))
        op("vector", lambda v: v.memset(zero_row[:], 0.0))
        op("vector", lambda v: v.memset(ones_col[:], 1.0))
        op("vector", lambda v: v.memset(kva[:], 0.0))
        op("vector", lambda v: v.memset(k_mk[:], 0.0))
        op("vector", lambda v: v.memset(c_eps5[:], LN_EPS))
        op("vector", lambda v: v.memset(c_eps6[:], ATT_EPS))
        op("vector", lambda v: v.memset(c_m1[:], -1.0), incs=[(s_dve, 1)])

        op("sync", lambda s: s.dma_start(out=mask_rep[:], in_=maskh), incs=[(s_const, 16)])
        op("sync", lambda s: s.dma_start(out=ident_sb[:], in_=identh), incs=[(s_const, 16)])
        op("sync", lambda s: s.dma_start(out=bqk[:], in_=bqkh), incs=[(s_const, 16)])
        consts_ld = s_const.count

        for tt in range(TPB):
            op("sync" if tt % 2 == 0 else "scalar", lambda s, tt=tt: s.dma_start(
                out=x_sb[:, 0, tt, :], in_=xh[128 * tt:128 * (tt + 1), :]),
               incs=[(s_x[0], 16)])
        st["x_loaded"][0] = s_x[0].count

        op("gpsimd", lambda g: g.dma_start(out=bv_row, in_=ba[2048:3072].rearrange("(a f) -> a f", a=1)),
           waits=[(s_x[0], st["x_loaded"][0])], incs=[(s_w, 16)])
        op("gpsimd", lambda g: g.dma_start(out=bp_row, in_=bp.rearrange("(a f) -> a f", a=1)),
           incs=[(s_w, 16)])
        op("gpsimd", lambda g: g.dma_start(out=bf_row, in_=bf.rearrange("(a f) -> a f", a=1)),
           incs=[(s_w, 16)])
        w_bias = s_w.count
        for c in range(KC):
            op("gpsimd", lambda g, c=c: g.dma_start(out=W_v[:, c, :], in_=Wa[128 * c:128 * (c + 1), 2048:3072]),
               incs=[(s_wv, 16)])
        w_v_all = s_wv.count
        for c in range(KC):
            op("gpsimd", lambda g, c=c: g.dma_start(out=W_qk[:, c, 1024:2048],
                                                    in_=Wa[128 * c:128 * (c + 1), 1024:2048]),
               incs=[(s_wqk, 16)])
        w_qk_k = s_wqk.count
        for c in range(KC):
            op("gpsimd", lambda g, c=c: g.dma_start(out=W_qk[:, c, 0:1024],
                                                    in_=Wa[128 * c:128 * (c + 1), 0:1024]),
               incs=[(s_wqk, 16)])
        w_qk_all = s_wqk.count
        for c in range(KC):
            op("gpsimd", lambda g, c=c: g.dma_start(out=W_pr[:, c, :], in_=Wp[128 * c:128 * (c + 1), :]),
               incs=[(s_wpr, 16)])
        w_pr_all = s_wpr.count

        for z in range(8):
            op("tensor", lambda t, z=z: t.matmul(
                pkv[:, 128 * z:128 * (z + 1)], zero_row[0:1, 0:128], zero_row[0:1, 0:128],
                start=(z % 4 == 0), stop=False, skip_group_check=True),
               waits=[(s_dve, 1)] if z == 0 else [])

        for tt in range(TPB):
            t = TPB + tt
            op("sync", lambda s, tt=tt, t=t: s.dma_start(
                out=x_sb[:, 1, tt, :], in_=xh[128 * t:128 * (t + 1), :]),
               waits=[(s_wqk, w_qk_all)] if tt == 0 else [],
               incs=[(s_x[1], 16)])
        st["x_loaded"][1] = s_x[1].count

        def emit_stats(S, src_ap_fn, first_waits):
            for i in range(TPB):
                op("vector", lambda v, i=i: v.bn_stats(S["bn12"][:, i, 0:6], src_ap_fn(i)[:, 0:512]),
                   waits=first_waits if i == 0 else [])
                op("vector", lambda v, i=i: v.bn_stats(S["bn12"][:, i, 6:12], src_ap_fn(i)[:, 512:1024]),
                   incs=[(s_dve, 1)] if i == TPB - 1 else [])
            op("vector", lambda v, c=s_dve.count: v.wait_ge(s_dve.h, c))
            for i in range(TPB):
                op("vector", lambda v, i=i: v.bn_aggr(S["mvab"][:, i, 0:2], S["bn12"][:, i, 0:6]))
                op("vector", lambda v, i=i: v.bn_aggr(S["mvab"][:, i, 2:4], S["bn12"][:, i, 6:12]),
                   incs=[(s_dve, 1)] if i == TPB - 1 else [])
            op("vector", lambda v, c=s_dve.count: v.wait_ge(s_dve.h, c))
            op("vector", lambda v: v.tensor_tensor(S["mvs4"][:], S["mvab"][:, :, 0:2], S["mvab"][:, :, 2:4], ALU.add))
            op("vector", lambda v: v.tensor_tensor(S["mvd4"][:], S["mvab"][:, :, 0:2], S["mvab"][:, :, 2:4], ALU.subtract),
               incs=[(s_dve, 1)])
            op("vector", lambda v, c=s_dve.count: v.wait_ge(s_dve.h, c))
            op("vector", lambda v: v.tensor_tensor(S["mvp4"][:], S["mvd4"][:], S["mvd4"][:], ALU.mult))
            op("vector", lambda v: v.tensor_scalar(S["mean4"][:], S["mvs4"][:, :, 0], 0.5, None, ALU.mult))
            op("vector", lambda v: v.tensor_scalar(S["var4"][:], S["mvs4"][:, :, 1], 0.5, None, ALU.mult),
               incs=[(s_dve, 1)])
            op("vector", lambda v, c=s_dve.count: v.wait_ge(s_dve.h, c))
            op("vector", lambda v: v.scalar_tensor_tensor(
                S["var4"][:], S["mvp4"][:, :, 0], 0.25, S["var4"][:], ALU.mult, ALU.add), incs=[(s_dve, 1)])
            d_c = s_dve.count
            op("scalar", lambda s: s.activation(out=S["sqv"][:], in_=S["var4"][:], func=AF.Sqrt, bias=LN_EPS),
               waits=[(s_dve, d_c)], incs=[(s_act, 1)])
            S["sqrt_act"] = s_act.count
            op("vector", lambda v: v.reciprocal(S["invs"][:], S["sqv"][:]),
               waits=[(s_act, S["sqrt_act"])], incs=[(s_dve, 1)])
            op("vector", lambda v, c=s_dve.count: v.wait_ge(s_dve.h, c))
            op("vector", lambda v: v.scalar_tensor_tensor(
                S["nmb"][:], S["mean4"][:], -1.0, S["invs"][:], ALU.mult, ALU.mult),
               incs=[(s_dve, 1)])
            return s_dve.count

        def emit_ln1_stats(b):
            bpar = b % 2
            S = stats[0]
            fw = [(s_x[bpar], st["x_loaded"][b])]
            if b > 1:
                fw.append((s_act, S["sqrt_act"]))
            d_c = emit_stats(S, lambda i, bpar=bpar: x_sb[:, bpar, i, :], fw)
            st["ln_apply_t"] = []
            for i in range(TPB):
                waits = [(s_dve, d_c)]
                if b > 0 and i == 0:
                    waits.append(st["ln_tm_free"][b - 1])
                op("scalar", lambda s, i=i, bpar=bpar, S=S: s.activation(
                    out=ln_tm[:, i, :], in_=x_sb[:, bpar, i, :], func=AF.Identity,
                    bias=S["nmb"][:, i:i + 1], scale=S["invs"][:, i:i + 1]),
                   waits=waits, incs=[(s_act, 1)])
                st["ln_apply_t"].append(s_act.count)
            st["ln_apply_act"] = s_act.count

        def emit_ln1_tr(b):
            if b == 0:
                for i in range(TPB):
                    for half in range(2):
                        pb = pbig[half]
                        for c4 in range(4):
                            c = half * 4 + c4
                            w = []
                            if c4 == 0:
                                w = [(s_act, st["ln_apply_t"][i]),
                                     (s_const, consts_ld)] + bank_free[half]
                            op("tensor", lambda t, i=i, c=c, c4=c4, pb=pb: t.matmul(
                                pb[:, 128 * c4:128 * (c4 + 1)],
                                ln_tm[:, i, 128 * c:128 * (c + 1)], ident_sb[:],
                                start=True, stop=True),
                               waits=w, incs=[(s_mm, 1)] if c4 == 3 else [])
                        mm_c = s_mm.count
                        op("vector", lambda v, i=i, half=half, pb=pb: v.tensor_copy(
                            ln_fm[:, i, 4 * half:4 * half + 4, :],
                            pb[:, 0:512].rearrange("p (a m) -> p a m", a=4)),
                           waits=[(s_mm, mm_c)], incs=[(s_dve, 1)])
                        bank_free[half] = [(s_dve, s_dve.count)]
                st["ln_fm_rdy"][0] = (s_dve, s_dve.count)
                st["ln_tm_free"][0] = (s_mm, s_mm.count)
                return
            for i in range(TPB):
                twaits = [(s_act, st["ln_apply_act"])]
                if i == 0:
                    twaits.append((s_mm, st["qk_mm"][b - 1]))
                op("scalar", lambda s, i=i: s.dma_start_transpose(
                    ln_fm[:, i, :, :], ln_tm[:, i, :].rearrange("p (c m) -> p c m", c=KC)),
                   waits=twaits if i == 0 else [], incs=[(s_trl, 16)])
            st["ln_fm_rdy"][b] = (s_trl, s_trl.count)
            st["ln_tm_free"][b] = (s_trl, s_trl.count)

        def big_nops():
            op("tensor", lambda t: t.nop())
            op("tensor", lambda t: t.nop())

        def emit_v(b):
            for i in range(TPB):
                for half in range(2):
                    bank = (i * 2 + half) % 2
                    waits = [st["ln_fm_rdy"][b], (s_w, w_bias), (s_wv, w_v_all)] + bank_free[bank]
                    for kc in range(KC):
                        op("tensor", lambda t, i=i, half=half, kc=kc, bank=bank: t.matmul(
                            pbig[bank][:], ln_fm[:, i, kc, :], W_v[:, kc, 512 * half:512 * (half + 1)],
                            start=(kc == 0), stop=False),
                           waits=waits if kc == 0 else [])
                        big_nops()
                    op("tensor", lambda t, half=half, bank=bank: t.matmul(
                        pbig[bank][:], ones_row[0:1, 0:128], bv_row[:, 512 * half:512 * (half + 1)],
                        start=False, stop=True), incs=[(s_mm, 1)])
                    mm_c = s_mm.count
                    big_nops()
                    ewaits = [(s_mm, mm_c)]
                    if b > 0 and i == 0 and half == 0:
                        ewaits.append((s_mm, st["attn_mm"][b - 1]))
                    op("vector", lambda v, i=i, half=half, bank=bank: v.tensor_copy(
                        v_sb[:, i, :].rearrange("p (h w) -> p h w", w=65)[:, 8 * half:8 * (half + 1), 0:64],
                        pbig[bank][:].rearrange("p (h d) -> p h d", d=64)),
                       waits=ewaits, incs=[(s_dve, 1)])
                    bank_free[bank] = [(s_dve, s_dve.count)]
            for i in range(TPB):
                op("vector", lambda v, i=i: v.memset(
                    v_sb[:, i, :].rearrange("p (h w) -> p h w", w=65)[:, :, 64:65], 1.0),
                   waits=[(s_mm, st["attn_mm"][b - 1])] if (b > 0 and i == 0) else [],
                   incs=[(s_dve, 1)] if i == TPB - 1 else [])
            st["v_ones_dve"] = s_dve.count
            st["v_ready"][b] = s_dve.count

        def emit_qk(b):
            for fc in [8, 9, 10, 11, 12, 13, 14, 15, 0, 1, 2, 3, 4, 5, 6, 7]:
                bank = fc % 2
                waits = [st["ln_fm_rdy"][b],
                         (s_wqk, w_qk_k if fc >= 8 else w_qk_all)] + bank_free[bank]
                for kc in range(KC):
                    op("tensor", lambda t, fc=fc, kc=kc, bank=bank: t.matmul(
                        pbig[bank][:], W_qk[:, kc, 128 * fc:128 * (fc + 1)], ln_fm[:, :, kc, :],
                        start=(kc == 0), stop=(kc == KC - 1)),
                       waits=waits if kc == 0 else [],
                       incs=[(s_mm, 1)] if kc == KC - 1 else [])
                    big_nops()
                mm_c = s_mm.count
                dst = q_fm[:, :, fc, :] if fc < 8 else k_fm[:, :, fc - 8, :]
                ep = fc % 2
                ewaits = [(s_mm, mm_c)]
                if len(st["phi_hist"]) >= 2:
                    ewaits.append((s_pl, st["phi_hist"][-2]))
                op("scalar", lambda s, fc=fc, ep=ep, bank=bank: s.activation(
                    out=e_sb[:, ep, :], in_=pbig[bank][:], func=AF.Exp, bias=bqk[:, fc:fc + 1]),
                   waits=ewaits, incs=[(s_act, 1)])
                a_c = s_act.count
                rw = [(s_act, a_c)]
                if b > 0 and fc == 8:
                    rw.append((s_mm, st["attn_mm"][b - 1]))
                op("vector", lambda v, fc=fc, dst=dst, bank=bank: v.tensor_scalar(
                    dst, pbig[bank][:].rearrange("p (a m) -> p a m", a=TPB),
                    bqk[:, fc:fc + 1], 0.0, ALU.add, ALU.max),
                   waits=rw, incs=[(s_dve, 1)])
                d_c = s_dve.count
                bank_free[bank] = [(s_dve, d_c)]
                op("vector", lambda v, ep=ep: v.tensor_scalar(
                    e_sb[:, ep, :], e_sb[:, ep, :], 1.0, None, ALU.min),
                   waits=[(s_act, a_c)], incs=[(s_dve, 1)])
                op("vector", lambda v, c=s_dve.count: v.wait_ge(s_dve.h, c))
                op("vector", lambda v, ep=ep, dst=dst: v.tensor_tensor(
                    dst, e_sb[:, ep, :].rearrange("p (a m) -> p a m", a=TPB),
                    dst, ALU.add),
                   incs=[(s_pl, 1)])
                st["phi_hist"].append(s_pl.count)
                if fc == 15:
                    st["k_ready"][b] = s_pl.count
            st["qk_ready"][b] = s_pl.count
            st["qk_mm"][b] = s_mm.count

        def emit_ktm(b):
            for i in range(TPB):
                waits = [(s_pl, st["k_ready"][b])]
                if b > 0:
                    if i == 0:
                        waits.append((s_mm, st["attn_mm"][b - 1]))
                    waits.append((s_trq, st["q8_tr_t"][(b - 1) * TPB + i]))
                op("scalar", lambda s, i=i: s.dma_start_transpose(
                    k_tm[:, i, :].rearrange("p (c m) -> p c m", c=KC), k_fm[:, i, :, :]),
                   waits=waits, incs=[(s_trk, 16)])
            st["ktm_tr"][b] = s_trk.count

        def emit_w8():
            w8_wide = []
            for c in range(KC):
                sl = c % 4
                if c == 0:
                    dw = [st["ln_fm_rdy"][0], (s_wqk, w_qk_all)]
                elif c >= 4:
                    dw = [(s_w8, w8_wide[c - 4])]
                else:
                    dw = []
                op("gpsimd", lambda g, c=c, sl=sl: g.dma_start(
                    out=q8_tm[:, sl, :].bitcast(FP8), in_=Wft[128 * c:128 * (c + 1), :]),
                   waits=dw, incs=[(s_stg[c % 2], 16)])
                stg_c = s_stg[c % 2].count
                op("gpsimd", lambda g, c=c, sl=sl: g.dma_start(
                    out=W_f8[:, c, :], in_=q8_tm[:, sl, :].bitcast(FP8)),
                   waits=[(s_stg[c % 2], stg_c)], incs=[(s_w8, 16)])
                w8_wide.append(s_w8.count)
            st["w8_all_pl"] = s_w8.count

        def emit_attn(b):
            for ci in range(TPB):
                c = b * TPB + ci
                par = c % 2
                for h in range(H):
                    po = 64 * (h % 2)
                    kvw = []
                    if h == 0:
                        kvw = [(s_dve, st["v_ready"][b]), (s_trk, st["ktm_tr"][b])]
                        if c > 0:
                            kvw.append((s_act, st["kva_act"]))
                    op("tensor", lambda t, h=h, po=po, ci=ci: t.matmul(
                        pkv[po:po + 64, _koff(h // 2):_koff(h // 2) + 65],
                        k_tm[:, ci, 64 * h:64 * (h + 1)],
                        v_sb[:, ci, 65 * h:65 * (h + 1)],
                        start=False, stop=False, skip_group_check=True),
                       waits=kvw, incs=[(s_mm, 1)] if h == H - 1 else [])
                kv_mm = s_mm.count
                op("scalar", lambda s, par=par: s.activation(
                    out=kva[:, par, 0:7, :], in_=pkv[:, 0:455].rearrange("p (a w) -> p a w", w=65),
                    func=AF.Copy),
                   waits=[(s_mm, kv_mm)])
                op("scalar", lambda s, par=par: s.activation(
                    out=kva[:, par, 7, :], in_=pkv[:, 512:577], func=AF.Copy),
                   incs=[(s_act, 1)])
                st["kva_act"] = s_act.count
                kmk_w = [(s_pl, st["k_ready"][b])]
                if c > 0:
                    kmk_w.append((s_mm, st["quad_mm"]))
                op("vector", lambda v, ci=ci: v.tensor_copy(
                    k_mk[0:64, 0, :, :], k_fm[0:64, ci, :, :]), waits=kmk_w)
                op("vector", lambda v, ci=ci: v.tensor_copy(
                    k_mk[64:128, 1, :, :], k_fm[64:128, ci, :, :]), incs=[(s_dve, 1)])
                kmk_dve = s_dve.count
                mask_dve = {}
                for qd in range(4):
                    bank = qd % 2
                    waits = [(s_dve, kmk_dve), (s_const, consts_ld)] + bank_free[bank]
                    for hh in range(4):
                        h = qd * 4 + hh
                        op("tensor", lambda t, h=h, hh=hh, ci=ci, bank=bank: t.matmul(
                            pbig[bank][:, 128 * hh:128 * (hh + 1)],
                            k_mk[:, h % 2, h // 2, :], q_fm[:, ci, h // 2, :],
                            start=(hh == 0), stop=(hh == 3)),
                           waits=waits if hh == 0 else [],
                           incs=[(s_mm, 1)] if hh == 3 else [])
                    st_mm = s_mm.count
                    op("vector", lambda v, qd=qd, bank=bank: v.scalar_tensor_tensor(
                        stm[:, 4 * qd:4 * (qd + 1), :], pbig[bank][:].rearrange("p (a m) -> p a m", a=4),
                        1.0, mask_rep[:, None, :].broadcast_to((128, 4, 128)), ALU.mult, ALU.mult),
                       waits=[(s_mm, st_mm)], incs=[(s_dve, 1)])
                    mask_dve[qd] = s_dve.count
                    bank_free[bank] = [(s_dve, mask_dve[qd])]
                st["quad_mm"] = s_mm.count
                ppar = (c - 1) % 2
                for h in range(H):
                    po = 64 * (h % 2)
                    no = _noff(h)
                    waits = [(s_dve, mask_dve[h // 4])]
                    op("tensor", lambda t, h=h, ci=ci, no=no: t.matmul(
                        pnd[:, no:no + 65], stm[:, h, :],
                        v_sb[:, ci, 65 * h:65 * (h + 1)],
                        start=(h in (0, 7, 14)), stop=False), waits=waits)
                    op("tensor", lambda t, h=h, po=po, ci=ci, no=no, ppar=ppar: t.matmul(
                        pnd[:, no:no + 65], q_fm[po:po + 64, ci, h // 2, :],
                        kva[po:po + 64, ppar, h // 2, 0:D + 1], start=False,
                        stop=(h in (6, 13, 15))),
                       incs=[(s_mm, 1)] if h == H - 1 else [])
                st["chunk_mm"] = s_mm.count
                st["attn_mm"][b] = st["chunk_mm"]
                waits = [(s_mm, st["chunk_mm"])]
                for g, (h0, nh) in enumerate(((0, 7), (7, 7), (14, 2))):
                    op("vector", lambda v, h0=h0, nh=nh, g=g: v.tensor_scalar(
                        den_f[:, h0:h0 + nh],
                        pnd[:, _noff(h0):_noff(h0) + 65 * nh].rearrange(
                            "p (a w) -> p a w", w=65)[:, :, 64],
                        ATT_EPS, None, ALU.add),
                       waits=waits if g == 0 else [],
                       incs=[(s_dve, 1)] if g == 2 else [])
                op("vector", lambda v, c_=s_dve.count: v.wait_ge(s_dve.h, c_))
                op("vector", lambda v: v.reciprocal(rcp_f[:], den_f[:]), incs=[(s_dve, 1)])
                rcp_dve = s_dve.count
                ywaits = [(s_dve, rcp_dve)]
                if c >= 1:
                    ywaits.append((s_try, st["ytr"][c - 1]))
                for g, (h0, nh) in enumerate(((0, 7), (7, 7), (14, 2))):
                    op("vector", lambda v, h0=h0, nh=nh: v.tensor_tensor(
                        y_tm[:, 64 * h0:64 * (h0 + nh)].rearrange("p (a d) -> p a d", d=64),
                        pnd[:, _noff(h0):_noff(h0) + 65 * nh].rearrange(
                            "p (a w) -> p a w", w=65)[:, :, 0:64],
                        rcp_f[:, h0:h0 + nh, None].broadcast_to((128, nh, D)), ALU.mult),
                       waits=ywaits if g == 0 else [],
                       incs=[(s_dve, 1)] if g == 2 else [])
                div_dve = s_dve.count
                st["div_dve"] = div_dve
                if ci > 0:
                    emit_proj_tile(b, ci - 1)
                ytw = [(s_dve, div_dve)]
                if c > 0:
                    ytw.append((s_mm, st["ptile_mm"][c - 1]))
                op("scalar", lambda s: s.dma_start_transpose(
                    y_fm1[:], y_tm[:].rearrange("p (c m) -> p c m", c=KC)),
                   waits=ytw, incs=[(s_try, 16)])
                st["ytr"][c] = s_try.count

        def emit_proj_tile(b, i):
            bpar = b % 2
            if True:
                for half in range(2):
                    bank = (i * 2 + half) % 2
                    waits = [(s_try, st["ytr"][b * TPB + i]), (s_w, w_bias),
                             (s_wpr, w_pr_all)] + bank_free[bank]
                    for kc in range(KC):
                        op("tensor", lambda t, i=i, half=half, kc=kc, bank=bank: t.matmul(
                            pbig[bank][:], y_fm1[:, kc, :], W_pr[:, kc, 512 * half:512 * (half + 1)],
                            start=(kc == 0), stop=False),
                           waits=waits if kc == 0 else [])
                        big_nops()
                    op("tensor", lambda t, half=half, bank=bank: t.matmul(
                        pbig[bank][:], ones_row[32:33, 0:128], bp_row[:, 512 * half:512 * (half + 1)],
                        start=False, stop=True), incs=[(s_mm, 1)])
                    mm_c = s_mm.count
                    big_nops()
                    ewaits = [(s_mm, mm_c)]
                    if b > 0 and i == 0 and half == 0:
                        ewaits.append((s_out, st["out_done"][b - 1]))
                    op("vector", lambda v, i=i, half=half, bank=bank, bpar=bpar: v.tensor_tensor(
                        r_sb[:, i, 512 * half:512 * (half + 1)], pbig[bank][:],
                        x_sb[:, bpar, i, 512 * half:512 * (half + 1)], ALU.add),
                       waits=ewaits, incs=[(s_dve, 1)])
                    bank_free[bank] = [(s_dve, s_dve.count)]
            st["ptile_mm"][b * TPB + i] = s_mm.count
            st["proj_mm"][b] = s_mm.count
            st.setdefault("proj_evac_dve", {})[b] = s_dve.count

        def emit_ln2q8(b):
            S = stats[1]
            fw = []
            if b > 0:
                fw.append((s_act, S["sqrt_act"]))
            d_c = emit_stats(S, lambda i: r_sb[:, i, :], fw)
            for i in range(TPB):
                qw = [(s_dve, d_c)] if i == 0 else []
                if b == 0 and i == 0:
                    qw.append((s_w8, st["w8_all_pl"]))
                op("vector", lambda v, i=i, S=S: v.tensor_scalar(
                    q8_tm[:, i, :].bitcast(FP8), r_sb[:, i, :],
                    S["invs"][:, i:i + 1], S["nmb"][:, i:i + 1], ALU.mult, ALU.add),
                   waits=qw, incs=[(s_dve, 1)])
                a_c = s_dve.count
                cw = [(s_dve, a_c)]
                if i == 0:
                    cw.append((s_mm, st["attn_mm"][b]))
                op("vector", lambda v, i=i: v.tensor_copy(k_tm[:, i, 0:C], q8_tm[:, i, :].bitcast(FP8)),
                   waits=cw, incs=[(s_dve, 1)])
                st["q8_cp"][b * TPB + i] = s_dve.count
            st["q8_act"][b] = s_dve.count

        def emit_ffn(b):
            bpar = b % 2
            for i in range(TPB):
                g = b * TPB + i
                tqw = [(s_dve, st["q8_cp"][g])]
                if g >= 2:
                    tqw.append((s_mm, st["ffn_mm_t"][g - 2]))
                op("scalar", lambda s, i=i: s.dma_start_transpose(
                    q8_fm[:, i % 2, :, :],
                    k_tm[:, i, 0:C].rearrange("p (c m) -> p c m", c=KC)),
                   waits=tqw, incs=[(s_trq, 16)])
                tr_c = s_trq.count
                st["q8_tr_t"][g] = tr_c
                for half in range(2):
                    bank = (i * 2 + half) % 2
                    waits = [(s_trq, tr_c), (s_w8, st["w8_all_pl"]),
                             (s_w, w_bias)] + bank_free[bank]
                    for kc in range(KC):
                        op("tensor", lambda t, i=i, half=half, kc=kc, bank=bank: t.matmul(
                            pbig[bank][:], q8_fm[:, i % 2, kc, :],
                            W_f8[:, kc, 512 * half:512 * (half + 1)],
                            start=(kc == 0), stop=False),
                           waits=waits if kc == 0 else [])
                        big_nops()
                    op("tensor", lambda t, half=half, bank=bank: t.matmul(
                        pbig[bank][:], ones_row[64:65, 0:128], bf_row[:, 512 * half:512 * (half + 1)],
                        start=False, stop=True), incs=[(s_mm, 1)])
                    mm_c = s_mm.count
                    big_nops()
                    ew = [(s_mm, mm_c)]
                    if i == 0 and half == 0:
                        ew.append((s_dve, st["q8_act"][b]))
                    op("vector", lambda v, i=i, half=half, bank=bank: v.tensor_tensor(
                        r_sb[:, i, 512 * half:512 * (half + 1)], pbig[bank][:],
                        r_sb[:, i, 512 * half:512 * (half + 1)], ALU.add),
                       waits=ew, incs=[(s_dve, 1)])
                    bank_free[bank] = [(s_dve, s_dve.count)]
                st["ffn_mm_t"][g] = s_mm.count
                d_c = s_dve.count
                t = b * TPB + i
                op("sync", lambda s, i=i, t=t: s.dma_start(
                    out=outh[128 * t:128 * (t + 1), :], in_=r_sb[:, i, :]),
                   waits=[(s_dve, d_c)], incs=[(s_out, 16)])
            st["ffn_mm"][b] = s_mm.count
            st["q8_tr"][b] = s_trq.count
            st["out_done"][b] = s_out.count

        def emit_prefetch(b):
            if b + 2 <= NBLK - 1:
                bn = b + 2
                for tt in range(TPB):
                    t = bn * TPB + tt
                    op("sync", lambda s, bn=bn, tt=tt, t=t: s.dma_start(
                        out=x_sb[:, bn % 2, tt, :], in_=xh[128 * t:128 * (t + 1), :]),
                       waits=[(s_dve, st["proj_evac_dve"][b])] if tt == 0 else [],
                       incs=[(s_x[bn % 2], 16)])
                st["x_loaded"][bn] = s_x[bn % 2].count

        emit_ln1_stats(0)
        emit_ln1_tr(0)
        emit_v(0)
        emit_qk(0)
        emit_ktm(0)
        for b in range(NBLK):
            if b + 1 < NBLK:
                emit_ln1_stats(b + 1)
            if b == 0:
                emit_w8()
            emit_attn(b)
            emit_proj_tile(b, TPB - 1)
            emit_prefetch(b)
            if b + 1 < NBLK:
                emit_ln1_tr(b + 1)
                emit_v(b + 1)
            emit_ln2q8(b)
            if b + 1 < NBLK:
                emit_qk(b + 1)
            emit_ffn(b)
            if b + 1 < NBLK:
                emit_ktm(b + 1)

        op("sync", lambda s: None, waits=[(s_out, s_out.count)])

        with nc.Block() as blk:
            def emit(eng_obj, name):
                for waits, fn, incs in B.eng_ops[name]:
                    for s, v in waits:
                        eng_obj.wait_ge(s.h, v)
                    inst = fn(eng_obj)
                    for jj, (s, n) in enumerate(incs):
                        assert jj == 0 and inst is not None
                        inst.then_inc(s.h, n)

            @blk.sync
            def _(e):
                emit(e, "sync")

            @blk.gpsimd
            def _(e):
                emit(e, "gpsimd")

            @blk.scalar
            def _(e):
                emit(e, "scalar")

            @blk.vector
            def _(e):
                emit(e, "vector")

            @blk.tensor
            def _(e):
                emit(e, "tensor")

    return nc


_NC_CACHE = None


def kernel(L0_attn_x, L0_attn_W_attn, L0_attn_b_attn, L0_attn_W_proj,
           L0_attn_b_proj, L0_ffn_weight, L0_ffn_bias):
    global _NC_CACHE
    x = np.ascontiguousarray(np.asarray(L0_attn_x, dtype=np.float32))
    Wa = np.ascontiguousarray(np.asarray(L0_attn_W_attn, dtype=np.float32))
    ba = np.ascontiguousarray(np.asarray(L0_attn_b_attn, dtype=np.float32))
    Wp = np.ascontiguousarray(np.asarray(L0_attn_W_proj, dtype=np.float32))
    bp = np.ascontiguousarray(np.asarray(L0_attn_b_proj, dtype=np.float32))
    Wf = np.ascontiguousarray(np.asarray(L0_ffn_weight, dtype=np.float32))
    bfv = np.ascontiguousarray(np.asarray(L0_ffn_bias, dtype=np.float32))
    Wft = np.ascontiguousarray(Wf.T)
    mask = np.ascontiguousarray(np.triu(np.ones((128, 128), np.float32)).astype(ml_dtypes.bfloat16))
    ident = np.ascontiguousarray(np.eye(128, dtype=np.float32).astype(ml_dtypes.bfloat16))
    bqk_h = np.ascontiguousarray(ba[:2048].reshape(16, 128).T)

    if _NC_CACHE is None:
        _NC_CACHE = build_nc()
    nc = _NC_CACHE
    in_maps = []
    for b in range(8):
        in_maps.append({"x": np.ascontiguousarray(x[b]), "W_attn": Wa, "b_attn": ba,
                        "W_proj": Wp, "b_proj": bp, "Wt_ffn": Wft, "b_ffn": bfv,
                        "mask": mask, "ident": ident, "bqk_h": bqk_h})
    res = run_bass_kernel_spmd(nc, in_maps, core_ids=list(range(8)))
    out = np.stack([np.asarray(res.results[b]["out"]) for b in range(8)], axis=0)
    return out.astype(np.float32)


# version 22 (final)
# speedup vs baseline: 1.0270x; 1.0270x over previous
import numpy as np
import ml_dtypes
import concourse.bass as bass
import concourse.mybir as mybir
from concourse.bass_utils import run_bass_kernel_spmd

F32 = mybir.dt.float32
BF16 = mybir.dt.bfloat16
FP8 = mybir.dt.float8e4
I8 = mybir.dt.int8
AF = mybir.ActivationFunctionType
ALU = mybir.AluOpType

T = 2048
C = 1024
H = 16
D = 64
KC = 8
NBLK = 4
TPB = 4
LN_EPS = 1e-5
ATT_EPS = 1e-6


class _Sem:
    def __init__(self, handle):
        self.h = handle
        self.count = 0


class _Builder:
    def __init__(self):
        self.eng_ops = {e: [] for e in ("sync", "scalar", "vector", "gpsimd", "tensor")}

    def op(self, eng, fn, waits=(), incs=()):
        for s, v in waits:
            assert v <= s.count, f"wait {v} > produced {s.count} ({eng})"
        self.eng_ops[eng].append((list(waits), fn, list(incs)))
        for s, n in incs:
            s.count += n


def build_nc():
    nc = bass.Bass("TRN2", target_bir_lowering=False, detect_race_conditions=False)

    xh = nc.dram_tensor("x", [T, C], F32, kind="ExternalInput").ap()
    Wa = nc.dram_tensor("W_attn", [C, 3 * C], F32, kind="ExternalInput").ap()
    ba = nc.dram_tensor("b_attn", [3 * C], F32, kind="ExternalInput").ap()
    Wp = nc.dram_tensor("W_proj", [C, C], F32, kind="ExternalInput").ap()
    bp = nc.dram_tensor("b_proj", [C], F32, kind="ExternalInput").ap()
    Wft = nc.dram_tensor("Wt_ffn", [C, C], F32, kind="ExternalInput").ap()
    bf = nc.dram_tensor("b_ffn", [C], F32, kind="ExternalInput").ap()
    maskh = nc.dram_tensor("mask", [128, 128], BF16, kind="ExternalInput").ap()
    identh = nc.dram_tensor("ident", [128, 128], BF16, kind="ExternalInput").ap()
    bqkh = nc.dram_tensor("bqk_h", [128, 16], F32, kind="ExternalInput").ap()
    outh = nc.dram_tensor("out", [T, C], F32, kind="ExternalOutput").ap()

    import contextlib
    ctx = contextlib.ExitStack()
    sb = lambda name, shape, dt: ctx.enter_context(nc.sbuf_tensor(name, shape, dt))
    ps = lambda name, shape, dt: ctx.enter_context(nc.psum_tensor(name, shape, dt))
    sem = lambda name: _Sem(ctx.enter_context(nc.semaphore(name)))

    with ctx:
        W_qk = sb("W_qk", [128, KC, 2 * C], BF16)
        W_v = sb("W_v", [128, KC, C], BF16)
        W_pr = sb("W_pr", [128, KC, C], BF16)
        W_f8 = sb("W_f8", [128, KC, C], BF16)
        bqk = sb("bqk", [128, 16], F32)
        brows = sb("brows", [65, C], BF16)
        bv_row = brows[0:1, :]
        bp_row = brows[32:33, :]
        bf_row = brows[64:65, :]
        ones_row = sb("ones_row", [65, 128], BF16)
        ones_col = sb("ones_col", [128, 1], BF16)
        zero_row = sb("zero_row", [1, 128], BF16)
        mask_rep = sb("mask_rep", [128, 128], BF16)
        ident_sb = sb("ident_sb", [128, 128], BF16)
        x_sb = sb("x_sb", [128, 2, TPB, C], F32)
        r_sb = sb("r_sb", [128, TPB, C], F32)
        ln_tm = sb("ln_tm", [128, TPB, C], BF16)
        ln_fm = sb("ln_fm", [128, TPB, KC, 128], BF16)
        y_fm1 = sb("y_fm1", [128, KC, 128], BF16)
        q_fm = sb("q_fm", [128, TPB, KC, 128], BF16)
        k_fm = sb("k_fm", [128, TPB, KC, 128], BF16)
        k_tm = sb("k_tm", [128, TPB, C], BF16)
        q8_fm = sb("q8_fm", [128, 2, KC, 128], BF16)
        v_sb = sb("v_sb", [128, TPB, H * 65], BF16)
        k_mk = sb("k_mk", [128, 2, KC, 128], BF16)
        stm = sb("stm", [128, H, 128], BF16)
        y_tm = sb("y_tm", [128, C], BF16)
        q8_tm = sb("q8_tm", [128, TPB, C], I8)
        e_sb = sb("e_sb", [128, 2, 512], BF16)
        kva = sb("kva", [128, 2, H // 2, D + 1], BF16)
        stats = []
        for tag in ("a", "b"):
            stats.append(dict(
                bn12=sb(f"bn12{tag}", [128, TPB, 12], F32),
                mvab=sb(f"mvab{tag}", [128, TPB, 4], F32),
                mvs4=sb(f"mvs4{tag}", [128, TPB, 2], F32),
                mvd4=sb(f"mvd4{tag}", [128, TPB, 2], F32),
                mvp4=sb(f"mvp4{tag}", [128, TPB, 2], F32),
                mean4=sb(f"mean4{tag}", [128, TPB], F32),
                var4=sb(f"var4{tag}", [128, TPB], F32),
                sqv=sb(f"sqv{tag}", [128, TPB], F32),
                invs=sb(f"invs{tag}", [128, TPB], F32),
                nmb=sb(f"nmb{tag}", [128, TPB], F32),
                sqrt_act=0,
            ))
        den_f = sb("den_f", [128, 16], F32)
        rcp_f = sb("rcp_f", [128, 16], F32)
        c_eps5 = sb("c_eps5", [128, 1], F32)
        c_eps6 = sb("c_eps6", [128, 1], F32)
        c_m1 = sb("c_m1", [128, 1], F32)
        nc.const_aps.register(F32, LN_EPS, c_eps5[:])
        nc.const_aps.register(F32, ATT_EPS, c_eps6[:])
        nc.const_aps.register(F32, -1.0, c_m1[:])
        pbig = [ps("pbig0", [128, 512], F32), ps("pbig1", [128, 512], F32)]
        pnd = ps("pnd", [128, 1536], F32)
        pkv = ps("pkv", [128, 1024], F32)

        s_x = [sem("s_x0"), sem("s_x1")]
        s_const = sem("s_const")
        s_w = sem("s_w")
        s_wv = sem("s_wv")
        s_wqk = sem("s_wqk")
        s_wpr = sem("s_wpr")
        s_stg = [sem("s_stg0"), sem("s_stg1")]
        s_trl = sem("s_trl")
        s_trk = sem("s_trk")
        s_try = sem("s_try")
        s_trq = sem("s_trq")
        s_mm = sem("s_mm")
        s_act = sem("s_act")
        s_dve = sem("s_dve")
        s_pl = sem("s_pl")
        s_w8 = sem("s_w8")
        s_out = sem("s_out")

        B = _Builder()
        op = B.op

        def _koff(s):
            return 65 * s if s < 7 else 512

        def _noff(h):
            if h < 7:
                return 65 * h
            if h < 14:
                return 512 + 65 * (h - 7)
            return 1024 + 65 * (h - 14)

        st = dict(
            x_loaded={}, out_done={}, ln_fm_rdy={}, ln_tm_free={}, ktm_tr={}, ytr={},
            ptile_mm={}, qk_mm={}, q8_cp={}, ffn_mm_t={}, q8_tr_t={},
            qk_ready={}, k_ready={}, v_ready={}, q8_tr={}, q8_act={},
            attn_mm={}, proj_mm={}, ffn_mm={}, nm_dve={},
            phi_hist=[], div_dve=0, kva_act=0, ks_dve=0, ksum_read_act={},
            chunk_mm=0, w8_all=0, w8_act_all=0,
        )
        bank_free = [[], []]

        op("vector", lambda v: v.memset(ones_row[:], 1.0))
        op("vector", lambda v: v.memset(zero_row[:], 0.0))
        op("vector", lambda v: v.memset(ones_col[:], 1.0))
        op("vector", lambda v: v.memset(kva[:], 0.0))
        op("vector", lambda v: v.memset(k_mk[:], 0.0))
        op("vector", lambda v: v.memset(c_eps5[:], LN_EPS))
        op("vector", lambda v: v.memset(c_eps6[:], ATT_EPS))
        op("vector", lambda v: v.memset(c_m1[:], -1.0), incs=[(s_dve, 1)])

        op("sync", lambda s: s.dma_start(out=mask_rep[:], in_=maskh), incs=[(s_const, 16)])
        op("sync", lambda s: s.dma_start(out=ident_sb[:], in_=identh), incs=[(s_const, 16)])
        op("sync", lambda s: s.dma_start(out=bqk[:], in_=bqkh), incs=[(s_const, 16)])
        consts_ld = s_const.count

        for tt in range(TPB):
            op("sync" if tt % 2 == 0 else "scalar", lambda s, tt=tt: s.dma_start(
                out=x_sb[:, 0, tt, :], in_=xh[128 * tt:128 * (tt + 1), :]),
               incs=[(s_x[0], 16)])
        st["x_loaded"][0] = s_x[0].count

        op("gpsimd", lambda g: g.dma_start(out=bv_row, in_=ba[2048:3072].rearrange("(a f) -> a f", a=1)),
           waits=[(s_x[0], st["x_loaded"][0])], incs=[(s_w, 16)])
        op("gpsimd", lambda g: g.dma_start(out=bp_row, in_=bp.rearrange("(a f) -> a f", a=1)),
           incs=[(s_w, 16)])
        op("gpsimd", lambda g: g.dma_start(out=bf_row, in_=bf.rearrange("(a f) -> a f", a=1)),
           incs=[(s_w, 16)])
        w_bias = s_w.count
        for c in range(KC):
            op("gpsimd", lambda g, c=c: g.dma_start(out=W_v[:, c, :], in_=Wa[128 * c:128 * (c + 1), 2048:3072]),
               incs=[(s_wv, 16)])
        w_v_all = s_wv.count
        for c in range(KC):
            op("gpsimd", lambda g, c=c: g.dma_start(out=W_qk[:, c, 1024:2048],
                                                    in_=Wa[128 * c:128 * (c + 1), 1024:2048]),
               incs=[(s_wqk, 16)])
        w_qk_k = s_wqk.count
        for c in range(KC):
            op("gpsimd", lambda g, c=c: g.dma_start(out=W_qk[:, c, 0:1024],
                                                    in_=Wa[128 * c:128 * (c + 1), 0:1024]),
               incs=[(s_wqk, 16)])
        w_qk_all = s_wqk.count
        for c in range(KC):
            op("gpsimd", lambda g, c=c: g.dma_start(out=W_pr[:, c, :], in_=Wp[128 * c:128 * (c + 1), :]),
               incs=[(s_wpr, 16)])
        w_pr_all = s_wpr.count

        for z in range(8):
            op("tensor", lambda t, z=z: t.matmul(
                pkv[:, 128 * z:128 * (z + 1)], zero_row[0:1, 0:128], zero_row[0:1, 0:128],
                start=(z % 4 == 0), stop=False, skip_group_check=True),
               waits=[(s_dve, 1)] if z == 0 else [])

        for tt in range(TPB):
            t = TPB + tt
            op("sync", lambda s, tt=tt, t=t: s.dma_start(
                out=x_sb[:, 1, tt, :], in_=xh[128 * t:128 * (t + 1), :]),
               waits=[(s_wqk, w_qk_all)] if tt == 0 else [],
               incs=[(s_x[1], 16)])
        st["x_loaded"][1] = s_x[1].count

        def emit_stats(S, src_ap_fn, first_waits):
            for i in range(TPB):
                op("vector", lambda v, i=i: v.bn_stats(S["bn12"][:, i, 0:6], src_ap_fn(i)[:, 0:512]),
                   waits=first_waits if i == 0 else [])
                op("vector", lambda v, i=i: v.bn_stats(S["bn12"][:, i, 6:12], src_ap_fn(i)[:, 512:1024]),
                   incs=[(s_dve, 1)] if i == TPB - 1 else [])
            op("vector", lambda v, c=s_dve.count: v.wait_ge(s_dve.h, c))
            for i in range(TPB):
                op("vector", lambda v, i=i: v.bn_aggr(S["mvab"][:, i, 0:2], S["bn12"][:, i, 0:6]))
                op("vector", lambda v, i=i: v.bn_aggr(S["mvab"][:, i, 2:4], S["bn12"][:, i, 6:12]),
                   incs=[(s_dve, 1)] if i == TPB - 1 else [])
            op("vector", lambda v, c=s_dve.count: v.wait_ge(s_dve.h, c))
            op("vector", lambda v: v.tensor_tensor(S["mvs4"][:], S["mvab"][:, :, 0:2], S["mvab"][:, :, 2:4], ALU.add))
            op("vector", lambda v: v.tensor_tensor(S["mvd4"][:], S["mvab"][:, :, 0:2], S["mvab"][:, :, 2:4], ALU.subtract),
               incs=[(s_dve, 1)])
            op("vector", lambda v, c=s_dve.count: v.wait_ge(s_dve.h, c))
            op("vector", lambda v: v.tensor_tensor(S["mvp4"][:], S["mvd4"][:], S["mvd4"][:], ALU.mult))
            op("vector", lambda v: v.tensor_scalar(S["mean4"][:], S["mvs4"][:, :, 0], 0.5, None, ALU.mult))
            op("vector", lambda v: v.tensor_scalar(S["var4"][:], S["mvs4"][:, :, 1], 0.5, None, ALU.mult),
               incs=[(s_dve, 1)])
            op("vector", lambda v, c=s_dve.count: v.wait_ge(s_dve.h, c))
            op("vector", lambda v: v.scalar_tensor_tensor(
                S["var4"][:], S["mvp4"][:, :, 0], 0.25, S["var4"][:], ALU.mult, ALU.add), incs=[(s_dve, 1)])
            d_c = s_dve.count
            op("scalar", lambda s: s.activation(out=S["sqv"][:], in_=S["var4"][:], func=AF.Sqrt, bias=LN_EPS),
               waits=[(s_dve, d_c)], incs=[(s_act, 1)])
            S["sqrt_act"] = s_act.count
            op("vector", lambda v: v.reciprocal(S["invs"][:], S["sqv"][:]),
               waits=[(s_act, S["sqrt_act"])], incs=[(s_dve, 1)])
            op("vector", lambda v, c=s_dve.count: v.wait_ge(s_dve.h, c))
            op("vector", lambda v: v.scalar_tensor_tensor(
                S["nmb"][:], S["mean4"][:], -1.0, S["invs"][:], ALU.mult, ALU.mult),
               incs=[(s_dve, 1)])
            return s_dve.count

        def emit_ln1_stats(b):
            bpar = b % 2
            S = stats[0]
            fw = [(s_x[bpar], st["x_loaded"][b])]
            if b > 1:
                fw.append((s_act, S["sqrt_act"]))
            d_c = emit_stats(S, lambda i, bpar=bpar: x_sb[:, bpar, i, :], fw)
            st["ln_apply_t"] = []
            for i in range(TPB):
                waits = [(s_dve, d_c)]
                if b > 0 and i == 0:
                    waits.append(st["ln_tm_free"][b - 1])
                op("scalar", lambda s, i=i, bpar=bpar, S=S: s.activation(
                    out=ln_tm[:, i, :], in_=x_sb[:, bpar, i, :], func=AF.Identity,
                    bias=S["nmb"][:, i:i + 1], scale=S["invs"][:, i:i + 1]),
                   waits=waits, incs=[(s_act, 1)])
                st["ln_apply_t"].append(s_act.count)
            st["ln_apply_act"] = s_act.count

        def emit_ln1_tr(b):
            if b == 0:
                for i in range(TPB):
                    for half in range(2):
                        pb = pbig[half]
                        for c4 in range(4):
                            c = half * 4 + c4
                            w = []
                            if c4 == 0:
                                w = [(s_act, st["ln_apply_t"][i]),
                                     (s_const, consts_ld)] + bank_free[half]
                            op("tensor", lambda t, i=i, c=c, c4=c4, pb=pb: t.matmul(
                                pb[:, 128 * c4:128 * (c4 + 1)],
                                ln_tm[:, i, 128 * c:128 * (c + 1)], ident_sb[:],
                                start=True, stop=True),
                               waits=w, incs=[(s_mm, 1)] if c4 == 3 else [])
                        mm_c = s_mm.count
                        op("vector", lambda v, i=i, half=half, pb=pb: v.tensor_copy(
                            ln_fm[:, i, 4 * half:4 * half + 4, :],
                            pb[:, 0:512].rearrange("p (a m) -> p a m", a=4)),
                           waits=[(s_mm, mm_c)], incs=[(s_dve, 1)])
                        bank_free[half] = [(s_dve, s_dve.count)]
                st["ln_fm_rdy"][0] = (s_dve, s_dve.count)
                st["ln_tm_free"][0] = (s_mm, s_mm.count)
                return
            for i in range(TPB):
                twaits = [(s_act, st["ln_apply_act"])]
                if i == 0:
                    twaits.append((s_mm, st["qk_mm"][b - 1]))
                op("scalar", lambda s, i=i: s.dma_start_transpose(
                    ln_fm[:, i, :, :], ln_tm[:, i, :].rearrange("p (c m) -> p c m", c=KC)),
                   waits=twaits if i == 0 else [], incs=[(s_trl, 16)])
            st["ln_fm_rdy"][b] = (s_trl, s_trl.count)
            st["ln_tm_free"][b] = (s_trl, s_trl.count)

        def big_nops():
            op("tensor", lambda t: t.nop())
            op("tensor", lambda t: t.nop())

        def emit_v(b):
            for i in range(TPB):
                for half in range(2):
                    bank = (i * 2 + half) % 2
                    waits = [st["ln_fm_rdy"][b], (s_w, w_bias), (s_wv, w_v_all)] + bank_free[bank]
                    for kc in range(KC):
                        op("tensor", lambda t, i=i, half=half, kc=kc, bank=bank: t.matmul(
                            pbig[bank][:], ln_fm[:, i, kc, :], W_v[:, kc, 512 * half:512 * (half + 1)],
                            start=(kc == 0), stop=False),
                           waits=waits if kc == 0 else [])
                        big_nops()
                    op("tensor", lambda t, half=half, bank=bank: t.matmul(
                        pbig[bank][:], ones_row[0:1, 0:128], bv_row[:, 512 * half:512 * (half + 1)],
                        start=False, stop=True), incs=[(s_mm, 1)])
                    mm_c = s_mm.count
                    big_nops()
                    ewaits = [(s_mm, mm_c)]
                    if b > 0 and i == 0 and half == 0:
                        ewaits.append((s_mm, st["attn_mm"][b - 1]))
                    op("vector", lambda v, i=i, half=half, bank=bank: v.tensor_copy(
                        v_sb[:, i, :].rearrange("p (h w) -> p h w", w=65)[:, 8 * half:8 * (half + 1), 0:64],
                        pbig[bank][:].rearrange("p (h d) -> p h d", d=64)),
                       waits=ewaits, incs=[(s_dve, 1)])
                    bank_free[bank] = [(s_dve, s_dve.count)]
            for i in range(TPB):
                op("vector", lambda v, i=i: v.memset(
                    v_sb[:, i, :].rearrange("p (h w) -> p h w", w=65)[:, :, 64:65], 1.0),
                   waits=[(s_mm, st["attn_mm"][b - 1])] if (b > 0 and i == 0) else [],
                   incs=[(s_dve, 1)] if i == TPB - 1 else [])
            st["v_ones_dve"] = s_dve.count
            st["v_ready"][b] = s_dve.count

        def emit_qk(b):
            for fc in [8, 9, 10, 11, 12, 13, 14, 15, 0, 1, 2, 3, 4, 5, 6, 7]:
                bank = fc % 2
                waits = [st["ln_fm_rdy"][b],
                         (s_wqk, w_qk_k if fc >= 8 else w_qk_all)] + bank_free[bank]
                for kc in range(KC):
                    op("tensor", lambda t, fc=fc, kc=kc, bank=bank: t.matmul(
                        pbig[bank][:], W_qk[:, kc, 128 * fc:128 * (fc + 1)], ln_fm[:, :, kc, :],
                        start=(kc == 0), stop=(kc == KC - 1)),
                       waits=waits if kc == 0 else [],
                       incs=[(s_mm, 1)] if kc == KC - 1 else [])
                    big_nops()
                mm_c = s_mm.count
                dst = q_fm[:, :, fc, :] if fc < 8 else k_fm[:, :, fc - 8, :]
                ep = fc % 2
                ewaits = [(s_mm, mm_c)]
                if len(st["phi_hist"]) >= 2:
                    ewaits.append((s_pl, st["phi_hist"][-2]))
                op("scalar", lambda s, fc=fc, ep=ep, bank=bank: s.activation(
                    out=e_sb[:, ep, :], in_=pbig[bank][:], func=AF.Exp, bias=bqk[:, fc:fc + 1]),
                   waits=ewaits, incs=[(s_act, 1)])
                a_c = s_act.count
                rw = [(s_act, a_c)]
                if b > 0 and fc == 8:
                    rw.append((s_mm, st["attn_mm"][b - 1]))
                op("vector", lambda v, fc=fc, dst=dst, bank=bank: v.tensor_scalar(
                    dst, pbig[bank][:].rearrange("p (a m) -> p a m", a=TPB),
                    bqk[:, fc:fc + 1], 0.0, ALU.add, ALU.max),
                   waits=rw, incs=[(s_dve, 1)])
                d_c = s_dve.count
                bank_free[bank] = [(s_dve, d_c)]
                op("vector", lambda v, ep=ep: v.tensor_scalar(
                    e_sb[:, ep, :], e_sb[:, ep, :], 1.0, None, ALU.min),
                   waits=[(s_act, a_c)], incs=[(s_dve, 1)])
                op("vector", lambda v, c=s_dve.count: v.wait_ge(s_dve.h, c))
                op("vector", lambda v, ep=ep, dst=dst: v.tensor_tensor(
                    dst, e_sb[:, ep, :].rearrange("p (a m) -> p a m", a=TPB),
                    dst, ALU.add),
                   incs=[(s_pl, 1)])
                st["phi_hist"].append(s_pl.count)
                if fc == 15:
                    st["k_ready"][b] = s_pl.count
            st["qk_ready"][b] = s_pl.count
            st["qk_mm"][b] = s_mm.count

        def emit_ktm(b):
            for i in range(TPB):
                waits = [(s_pl, st["k_ready"][b])]
                if b > 0:
                    if i == 0:
                        waits.append((s_mm, st["attn_mm"][b - 1]))
                    waits.append((s_trq, st["q8_tr_t"][(b - 1) * TPB + i]))
                op("scalar", lambda s, i=i: s.dma_start_transpose(
                    k_tm[:, i, :].rearrange("p (c m) -> p c m", c=KC), k_fm[:, i, :, :]),
                   waits=waits, incs=[(s_trk, 16)])
            st["ktm_tr"][b] = s_trk.count

        def emit_w8():
            w8_wide = []
            for c in range(KC):
                sl = c % 4
                if c == 0:
                    dw = [st["ln_fm_rdy"][0], (s_wqk, w_qk_all)]
                elif c >= 4:
                    dw = [(s_w8, w8_wide[c - 4])]
                else:
                    dw = []
                op("gpsimd", lambda g, c=c, sl=sl: g.dma_start(
                    out=q8_tm[:, sl, :].bitcast(FP8), in_=Wft[128 * c:128 * (c + 1), :]),
                   waits=dw, incs=[(s_stg[c % 2], 16)])
                stg_c = s_stg[c % 2].count
                op("gpsimd", lambda g, c=c, sl=sl: g.dma_start(
                    out=W_f8[:, c, :], in_=q8_tm[:, sl, :].bitcast(FP8)),
                   waits=[(s_stg[c % 2], stg_c)], incs=[(s_w8, 16)])
                w8_wide.append(s_w8.count)
            st["w8_all_pl"] = s_w8.count

        def emit_attn(b):
            for ci in range(TPB):
                c = b * TPB + ci
                par = c % 2
                for h in range(H):
                    po = 64 * (h % 2)
                    kvw = []
                    if h == 0:
                        kvw = [(s_dve, st["v_ready"][b]), (s_trk, st["ktm_tr"][b])]
                        if c > 0:
                            kvw.append((s_act, st["kva_act"]))
                    op("tensor", lambda t, h=h, po=po, ci=ci: t.matmul(
                        pkv[po:po + 64, _koff(h // 2):_koff(h // 2) + 65],
                        k_tm[:, ci, 64 * h:64 * (h + 1)],
                        v_sb[:, ci, 65 * h:65 * (h + 1)],
                        start=False, stop=False, skip_group_check=True),
                       waits=kvw, incs=[(s_mm, 1)] if h == H - 1 else [])
                kv_mm = s_mm.count
                op("scalar", lambda s, par=par: s.activation(
                    out=kva[:, par, 0:7, :], in_=pkv[:, 0:455].rearrange("p (a w) -> p a w", w=65),
                    func=AF.Copy),
                   waits=[(s_mm, kv_mm)])
                op("scalar", lambda s, par=par: s.activation(
                    out=kva[:, par, 7, :], in_=pkv[:, 512:577], func=AF.Copy),
                   incs=[(s_act, 1)])
                st["kva_act"] = s_act.count
                kmk_w = [(s_pl, st["k_ready"][b])]
                if c > 0:
                    kmk_w.append((s_mm, st["quad_mm"]))
                op("vector", lambda v, ci=ci: v.tensor_copy(
                    k_mk[0:64, 0, :, :], k_fm[0:64, ci, :, :]), waits=kmk_w)
                op("vector", lambda v, ci=ci: v.tensor_copy(
                    k_mk[64:128, 1, :, :], k_fm[64:128, ci, :, :]), incs=[(s_dve, 1)])
                kmk_dve = s_dve.count
                mask_dve = {}
                for qd in range(4):
                    bank = qd % 2
                    waits = [(s_dve, kmk_dve), (s_const, consts_ld)] + bank_free[bank]
                    for hh in range(4):
                        h = qd * 4 + hh
                        op("tensor", lambda t, h=h, hh=hh, ci=ci, bank=bank: t.matmul(
                            pbig[bank][:, 128 * hh:128 * (hh + 1)],
                            k_mk[:, h % 2, h // 2, :], q_fm[:, ci, h // 2, :],
                            start=(hh == 0), stop=(hh == 3)),
                           waits=waits if hh == 0 else [],
                           incs=[(s_mm, 1)] if hh == 3 else [])
                    st_mm = s_mm.count
                    op("vector", lambda v, qd=qd, bank=bank: v.scalar_tensor_tensor(
                        stm[:, 4 * qd:4 * (qd + 1), :], pbig[bank][:].rearrange("p (a m) -> p a m", a=4),
                        1.0, mask_rep[:, None, :].broadcast_to((128, 4, 128)), ALU.mult, ALU.mult),
                       waits=[(s_mm, st_mm)], incs=[(s_dve, 1)])
                    mask_dve[qd] = s_dve.count
                    bank_free[bank] = [(s_dve, mask_dve[qd])]
                st["quad_mm"] = s_mm.count
                ppar = (c - 1) % 2
                for h in range(H):
                    po = 64 * (h % 2)
                    no = _noff(h)
                    waits = [(s_dve, mask_dve[h // 4])]
                    op("tensor", lambda t, h=h, ci=ci, no=no: t.matmul(
                        pnd[:, no:no + 65], stm[:, h, :],
                        v_sb[:, ci, 65 * h:65 * (h + 1)],
                        start=(h in (0, 7, 14)), stop=False), waits=waits)
                    op("tensor", lambda t, h=h, po=po, ci=ci, no=no, ppar=ppar: t.matmul(
                        pnd[:, no:no + 65], q_fm[po:po + 64, ci, h // 2, :],
                        kva[po:po + 64, ppar, h // 2, 0:D + 1], start=False,
                        stop=(h in (6, 13, 15))),
                       incs=[(s_mm, 1)] if h == H - 1 else [])
                st["chunk_mm"] = s_mm.count
                st["attn_mm"][b] = st["chunk_mm"]
                waits = [(s_mm, st["chunk_mm"])]
                for g, (h0, nh) in enumerate(((0, 7), (7, 7), (14, 2))):
                    op("vector", lambda v, h0=h0, nh=nh, g=g: v.tensor_scalar(
                        den_f[:, h0:h0 + nh],
                        pnd[:, _noff(h0):_noff(h0) + 65 * nh].rearrange(
                            "p (a w) -> p a w", w=65)[:, :, 64],
                        ATT_EPS, None, ALU.add),
                       waits=waits if g == 0 else [],
                       incs=[(s_dve, 1)] if g == 2 else [])
                op("vector", lambda v, c_=s_dve.count: v.wait_ge(s_dve.h, c_))
                op("vector", lambda v: v.reciprocal(rcp_f[:], den_f[:]), incs=[(s_dve, 1)])
                rcp_dve = s_dve.count
                ywaits = [(s_dve, rcp_dve)]
                if c >= 1:
                    ywaits.append((s_try, st["ytr"][c - 1]))
                for g, (h0, nh) in enumerate(((0, 7), (7, 7), (14, 2))):
                    op("vector", lambda v, h0=h0, nh=nh: v.tensor_tensor(
                        y_tm[:, 64 * h0:64 * (h0 + nh)].rearrange("p (a d) -> p a d", d=64),
                        pnd[:, _noff(h0):_noff(h0) + 65 * nh].rearrange(
                            "p (a w) -> p a w", w=65)[:, :, 0:64],
                        rcp_f[:, h0:h0 + nh, None].broadcast_to((128, nh, D)), ALU.mult),
                       waits=ywaits if g == 0 else [],
                       incs=[(s_dve, 1)] if g == 2 else [])
                div_dve = s_dve.count
                st["div_dve"] = div_dve
                if ci > 0:
                    emit_proj_tile(b, ci - 1)
                ytw = [(s_dve, div_dve)]
                if c > 0:
                    ytw.append((s_mm, st["ptile_mm"][c - 1]))
                op("scalar", lambda s: s.dma_start_transpose(
                    y_fm1[:], y_tm[:].rearrange("p (c m) -> p c m", c=KC)),
                   waits=ytw, incs=[(s_try, 16)])
                st["ytr"][c] = s_try.count

        def emit_proj_tile(b, i):
            bpar = b % 2
            if True:
                for half in range(2):
                    bank = (i * 2 + half) % 2
                    waits = [(s_try, st["ytr"][b * TPB + i]), (s_w, w_bias),
                             (s_wpr, w_pr_all)] + bank_free[bank]
                    for kc in range(KC):
                        op("tensor", lambda t, i=i, half=half, kc=kc, bank=bank: t.matmul(
                            pbig[bank][:], y_fm1[:, kc, :], W_pr[:, kc, 512 * half:512 * (half + 1)],
                            start=(kc == 0), stop=False),
                           waits=waits if kc == 0 else [])
                        big_nops()
                    op("tensor", lambda t, half=half, bank=bank: t.matmul(
                        pbig[bank][:], ones_row[32:33, 0:128], bp_row[:, 512 * half:512 * (half + 1)],
                        start=False, stop=True), incs=[(s_mm, 1)])
                    mm_c = s_mm.count
                    big_nops()
                    ewaits = [(s_mm, mm_c)]
                    if b > 0 and i == 0 and half == 0:
                        ewaits.append((s_out, st["out_done"][b - 1]))
                    op("vector", lambda v, i=i, half=half, bank=bank, bpar=bpar: v.tensor_tensor(
                        r_sb[:, i, 512 * half:512 * (half + 1)], pbig[bank][:],
                        x_sb[:, bpar, i, 512 * half:512 * (half + 1)], ALU.add),
                       waits=ewaits, incs=[(s_dve, 1)])
                    bank_free[bank] = [(s_dve, s_dve.count)]
            st["ptile_mm"][b * TPB + i] = s_mm.count
            st["proj_mm"][b] = s_mm.count
            st.setdefault("proj_evac_dve", {})[b] = s_dve.count

        def emit_ln2q8(b):
            S = stats[1]
            fw = []
            if b > 0:
                fw.append((s_act, S["sqrt_act"]))
            d_c = emit_stats(S, lambda i: r_sb[:, i, :], fw)
            for i in range(TPB):
                qw = [(s_dve, d_c)] if i == 0 else []
                if b == 0 and i == 0:
                    qw.append((s_w8, st["w8_all_pl"]))
                op("vector", lambda v, i=i, S=S: v.tensor_scalar(
                    q8_tm[:, i, :].bitcast(FP8), r_sb[:, i, :],
                    S["invs"][:, i:i + 1], S["nmb"][:, i:i + 1], ALU.mult, ALU.add),
                   waits=qw, incs=[(s_dve, 1)])
                a_c = s_dve.count
                cw = [(s_dve, a_c)]
                if i == 0:
                    cw.append((s_mm, st["attn_mm"][b]))
                op("gpsimd", lambda g, i=i: g.dma_start(
                    out=k_tm[:, i, 0:C], in_=q8_tm[:, i, :].bitcast(FP8)),
                   waits=cw, incs=[(s_w8, 16)])
                st["q8_cp"][b * TPB + i] = s_w8.count
            st["q8_act"][b] = s_dve.count

        def emit_ffn(b):
            bpar = b % 2
            for i in range(TPB):
                g = b * TPB + i
                tqw = [(s_w8, st["q8_cp"][g])]
                if g >= 2:
                    tqw.append((s_mm, st["ffn_mm_t"][g - 2]))
                op("scalar", lambda s, i=i: s.dma_start_transpose(
                    q8_fm[:, i % 2, :, :],
                    k_tm[:, i, 0:C].rearrange("p (c m) -> p c m", c=KC)),
                   waits=tqw, incs=[(s_trq, 16)])
                tr_c = s_trq.count
                st["q8_tr_t"][g] = tr_c
                for half in range(2):
                    bank = (i * 2 + half) % 2
                    waits = [(s_trq, tr_c), (s_w8, st["w8_all_pl"]),
                             (s_w, w_bias)] + bank_free[bank]
                    for kc in range(KC):
                        op("tensor", lambda t, i=i, half=half, kc=kc, bank=bank: t.matmul(
                            pbig[bank][:], q8_fm[:, i % 2, kc, :],
                            W_f8[:, kc, 512 * half:512 * (half + 1)],
                            start=(kc == 0), stop=False),
                           waits=waits if kc == 0 else [])
                        big_nops()
                    op("tensor", lambda t, half=half, bank=bank: t.matmul(
                        pbig[bank][:], ones_row[64:65, 0:128], bf_row[:, 512 * half:512 * (half + 1)],
                        start=False, stop=True), incs=[(s_mm, 1)])
                    mm_c = s_mm.count
                    big_nops()
                    ew = [(s_mm, mm_c)]
                    if i == 0 and half == 0:
                        ew.append((s_dve, st["q8_act"][b]))
                    op("vector", lambda v, i=i, half=half, bank=bank: v.tensor_tensor(
                        r_sb[:, i, 512 * half:512 * (half + 1)], pbig[bank][:],
                        r_sb[:, i, 512 * half:512 * (half + 1)], ALU.add),
                       waits=ew, incs=[(s_dve, 1)])
                    bank_free[bank] = [(s_dve, s_dve.count)]
                st["ffn_mm_t"][g] = s_mm.count
                d_c = s_dve.count
                t = b * TPB + i
                op("sync", lambda s, i=i, t=t: s.dma_start(
                    out=outh[128 * t:128 * (t + 1), :], in_=r_sb[:, i, :]),
                   waits=[(s_dve, d_c)], incs=[(s_out, 16)])
            st["ffn_mm"][b] = s_mm.count
            st["q8_tr"][b] = s_trq.count
            st["out_done"][b] = s_out.count

        def emit_prefetch(b):
            if b + 2 <= NBLK - 1:
                bn = b + 2
                for tt in range(TPB):
                    t = bn * TPB + tt
                    op("sync", lambda s, bn=bn, tt=tt, t=t: s.dma_start(
                        out=x_sb[:, bn % 2, tt, :], in_=xh[128 * t:128 * (t + 1), :]),
                       waits=[(s_dve, st["proj_evac_dve"][b])] if tt == 0 else [],
                       incs=[(s_x[bn % 2], 16)])
                st["x_loaded"][bn] = s_x[bn % 2].count

        emit_ln1_stats(0)
        emit_ln1_tr(0)
        emit_v(0)
        emit_qk(0)
        emit_ktm(0)
        for b in range(NBLK):
            if b + 1 < NBLK:
                emit_ln1_stats(b + 1)
            if b == 0:
                emit_w8()
            emit_attn(b)
            emit_proj_tile(b, TPB - 1)
            emit_prefetch(b)
            if b + 1 < NBLK:
                emit_ln1_tr(b + 1)
                emit_v(b + 1)
            emit_ln2q8(b)
            if b + 1 < NBLK:
                emit_qk(b + 1)
            emit_ffn(b)
            if b + 1 < NBLK:
                emit_ktm(b + 1)

        op("sync", lambda s: None, waits=[(s_out, s_out.count)])

        with nc.Block() as blk:
            def emit(eng_obj, name):
                for waits, fn, incs in B.eng_ops[name]:
                    for s, v in waits:
                        eng_obj.wait_ge(s.h, v)
                    inst = fn(eng_obj)
                    for jj, (s, n) in enumerate(incs):
                        assert jj == 0 and inst is not None
                        inst.then_inc(s.h, n)

            @blk.sync
            def _(e):
                emit(e, "sync")

            @blk.gpsimd
            def _(e):
                emit(e, "gpsimd")

            @blk.scalar
            def _(e):
                emit(e, "scalar")

            @blk.vector
            def _(e):
                emit(e, "vector")

            @blk.tensor
            def _(e):
                emit(e, "tensor")

    return nc


_NC_CACHE = None


def kernel(L0_attn_x, L0_attn_W_attn, L0_attn_b_attn, L0_attn_W_proj,
           L0_attn_b_proj, L0_ffn_weight, L0_ffn_bias):
    global _NC_CACHE
    x = np.ascontiguousarray(np.asarray(L0_attn_x, dtype=np.float32))
    Wa = np.ascontiguousarray(np.asarray(L0_attn_W_attn, dtype=np.float32))
    ba = np.ascontiguousarray(np.asarray(L0_attn_b_attn, dtype=np.float32))
    Wp = np.ascontiguousarray(np.asarray(L0_attn_W_proj, dtype=np.float32))
    bp = np.ascontiguousarray(np.asarray(L0_attn_b_proj, dtype=np.float32))
    Wf = np.ascontiguousarray(np.asarray(L0_ffn_weight, dtype=np.float32))
    bfv = np.ascontiguousarray(np.asarray(L0_ffn_bias, dtype=np.float32))
    Wft = np.ascontiguousarray(Wf.T)
    mask = np.ascontiguousarray(np.triu(np.ones((128, 128), np.float32)).astype(ml_dtypes.bfloat16))
    ident = np.ascontiguousarray(np.eye(128, dtype=np.float32).astype(ml_dtypes.bfloat16))
    bqk_h = np.ascontiguousarray(ba[:2048].reshape(16, 128).T)

    if _NC_CACHE is None:
        _NC_CACHE = build_nc()
    nc = _NC_CACHE
    in_maps = []
    for b in range(8):
        in_maps.append({"x": np.ascontiguousarray(x[b]), "W_attn": Wa, "b_attn": ba,
                        "W_proj": Wp, "b_proj": bp, "Wt_ffn": Wft, "b_ffn": bfv,
                        "mask": mask, "ident": ident, "bqk_h": bqk_h})
    res = run_bass_kernel_spmd(nc, in_maps, core_ids=list(range(8)))
    out = np.stack([np.asarray(res.results[b]["out"]) for b in range(8)], axis=0)
    return out.astype(np.float32)
